# version 15
# baseline (speedup 1.0000x reference)
import numpy as np
from contextlib import ExitStack

import concourse.bass as bass
import concourse.mybir as mybir
from concourse.bass_utils import run_bass_kernel_spmd

F32 = mybir.dt.float32
BF16 = mybir.dt.bfloat16
AF = mybir.ActivationFunctionType
OP = mybir.AluOpType
AXX = mybir.AxisListType.X

D = 1024
NCH = 8
H2 = 8
EPS = 1e-6
LNEPS = 1e-5
TT = 512
NSUB = 4
NXS = 4

ENGS = ("pe", "v", "s", "g", "sy")
WNAMES = ("wq", "wk", "wv", "wo", "wr", "wkf", "wvf")


def build_program(S, NT, dbg=False):
    nc = bass.Bass("TRN2", target_bir_lowering=False)

    x_h = nc.dram_tensor("x", [S, D], F32, kind="ExternalInput").ap()
    out_h = nc.dram_tensor("out", [S, D], F32, kind="ExternalOutput").ap()
    w_h = {nm: nc.dram_tensor(nm, [D, D], F32, kind="ExternalInput").ap()
           for nm in WNAMES}
    wbf_h = {nm: nc.dram_tensor("bf_" + nm, [D, D], BF16).ap()
             for nm in WNAMES}
    rw_h = nc.dram_tensor("rw", [16, 64, 64], F32, kind="ExternalInput").ap()
    b_h = {nm: nc.dram_tensor(nm, [D], F32, kind="ExternalInput").ap()
           for nm in ("bq", "bk", "bv", "bo", "mur", "muk")}

    with ExitStack() as ctx:
        e = ctx.enter_context
        semnames = ["pe", "v", "s", "g", "wt", "st0", "st1",
                    "ld0", "ld1", "ld2", "ld3", "xb0", "xb1", "xb2", "xb3",
                    "wl0", "wl1", "wl2", "wl3", "su"]
        sems = {nm: e(nc.semaphore(nm)) for nm in semnames}

        q = {E: [] for E in ENGS}
        cnt = {nm: 0 for nm in sems}
        waited = {E: {} for E in ENGS}
        OWN = {"pe": "pe", "v": "v", "s": "s", "g": "g"}

        def W(E, tok_):
            if tok_ is None:
                return
            sem, v = tok_
            if v <= 0 or OWN.get(sem) == E:
                return
            if waited[E].get(sem, 0) >= v:
                return
            waited[E][sem] = v
            q[E].append(lambda eng, s=sem, vv=v: eng.wait_ge(sems[s], vv))

        def Wm(E, toks):
            for t in toks:
                W(E, t)

        def G(E, body, sem=None, n=1):
            sname = sem or E
            def cl(eng, body=body, sname=sname, n=n):
                ins = body(eng)
                if ins is not None:
                    ins.then_inc(sems[sname], n)
            q[E].append(cl)
            cnt[sname] += n
            return (sname, cnt[sname])

        def Graw(E, body):
            q[E].append(body)

        def tok(sname):
            return (sname, cnt[sname])

        dbg_n = [0]

        def dump(name, ap, t):
            if not dbg:
                return
            hd = nc.dram_tensor("dbg_" + name, [128, ap.shape[-1]],
                                ap.dtype, kind="ExternalOutput").ap()
            W("g", t)
            def b_dump(g, hd=hd, ap=ap):
                return g.dma_start(hd, ap)
            t_d = G("g", b_dump, sem="wt", n=16)
            dbg_n[0] += 1
            for E in ("pe", "v", "s", "sy"):
                W(E, t_d)

        rwbd = e(nc.sbuf_tensor("rwbd", [128, H2, 128], BF16))
        bqf = e(nc.sbuf_tensor("bqf", [128, NCH], F32))
        bkf = e(nc.sbuf_tensor("bkf", [128, NCH], F32))
        murf = e(nc.sbuf_tensor("murf", [128, NCH], F32))
        mukf = e(nc.sbuf_tensor("mukf", [128, NCH], F32))
        idn = e(nc.sbuf_tensor("idn", [128, 128], BF16))
        ones128 = e(nc.sbuf_tensor("ones128", [128, 128], BF16))
        kvm = e(nc.sbuf_tensor("kvm", [128, H2, 128], BF16))
        ksumM = e(nc.sbuf_tensor("ksumM", [128, 16], BF16))
        lneps = e(nc.sbuf_tensor("lneps", [128, 1], F32))
        fence = e(nc.sbuf_tensor("fence", [1, 1], F32))
        stats = e(nc.sbuf_tensor("stats", [128, 64], F32))
        stall = e(nc.sbuf_tensor("stall", [128, NT, NSUB, 2], F32))
        recip = e(nc.sbuf_tensor("recip", [128, NSUB, 16], F32))
        junk = e(nc.sbuf_tensor("junk", [128, 2048], BF16))
        borow = junk[0:1, 1024:2048]
        bvrow = junk[32:33, 1024:2048]
        diag2 = e(nc.sbuf_tensor("diag2", [128, 2, NSUB, 128], BF16))
        diagn = e(nc.sbuf_tensor("diagn", [128, 2, NSUB, 128], BF16))
        h2d2 = e(nc.sbuf_tensor("h2d2", [128, 2, 128], BF16))
        h2dn = e(nc.sbuf_tensor("h2dn", [128, 2, 128], BF16))

        wpool = e(nc.sbuf_tensor("wpool", [128, 3, NCH, D], BF16))

        xA = e(nc.sbuf_tensor("xA", [128, NXS, D], F32))
        xbf = e(nc.sbuf_tensor("xbf", [128, NXS, D], BF16))
        h1fm = e(nc.sbuf_tensor("h1fm", [128, NCH, TT], BF16))
        kqbf = e(nc.sbuf_tensor("kqbf", [128, NCH, TT], BF16))
        qpf = e(nc.sbuf_tensor("qpf", [128, NCH, TT], BF16))
        A1 = e(nc.sbuf_tensor("A1", [128, NSUB, D], BF16))
        A2 = e(nc.sbuf_tensor("A2", [128, NSUB, H2, 129], BF16))
        dfm = e(nc.sbuf_tensor("dfm", [128, NCH, TT], BF16))
        xrb = e(nc.sbuf_tensor("xrb", [128, NCH, TT], BF16))
        ffm = e(nc.sbuf_tensor("ffm", [128, NCH, TT], BF16))
        h2fm = e(nc.sbuf_tensor("h2fm", [128, NCH, TT + 1], BF16))
        x2 = e(nc.sbuf_tensor("x2", [128, NSUB, D], F32))
        outS = e(nc.sbuf_tensor("outS", [128, 2, D], F32))
        AR2 = e(nc.sbuf_tensor("AR2", [128, 4096], BF16))
        vjunk = e(nc.sbuf_tensor("vjunk", [128, 1024], BF16))
        odivfm = AR2[:, 0:1024].rearrange("p (c t) -> p c t", c=NCH)
        x2bfb = AR2[:, 1024:3072].rearrange("p (a d) -> p a d", a=2)
        rbf = AR2[:, 0:4096].rearrange("p (c t) -> p c t", c=NCH)

        SC_SUM, SC_SQ, SC_M, SC_T, SC_VAR, SC_STD, SC_RSTD, SC_NEGM = range(8)

        def mkst(base, nsub):
            def st(col, s=None):
                b = base + col * nsub
                if s is None:
                    return stats[:, b:b + nsub]
                return stats[:, b + s:b + s + 1]
            return st
        stX = mkst(32, NSUB)
        stF = mkst(0, 2)
        stA = mkst(8, 1)
        stB = mkst(16, 1)

        PS = e(nc.psum_tensor("PS", [128, 4096], F32))
        trF32 = [PS[:, 0:512], PS[:, 3584:4096]]
        trBF = [PS[:, 0:512].bitcast(BF16), PS[:, 3584:4096].bitcast(BF16)]
        pk = [PS[:, 512:1024], PS[:, 1024:1536]]
        kvP = PS[:, 1536:3072]
        vP = PS[:, 3072:3584]
        kpP = PS[:, 3584:4096]
        KVCOL = [0, 129, 258, 512, 641, 770, 1024, 1153]
        KVSTART = {0, 3, 6}

        def bigP(sl):
            return PS[:, 1536 + sl * D: 1536 + (sl + 1) * D]

        pkq = PS[:, 512:1536]

        N_SETUP_DMA = 7 + 2 * H2 + 4 + 2

        def setup():
            def body(g):
                for nm in WNAMES:
                    g.dma_start(wbf_h[nm], w_h[nm]).then_inc(sems["wt"], 16)
                g.memset(rwbd[:, :, :], 0.0).then_inc(sems["su"], 1)
                g.wait_ge(sems["su"], 1)
                for c in range(H2):
                    g.dma_start(rwbd[0:64, c, 0:64],
                                rw_h[2 * c, :, :]).then_inc(sems["wt"], 16)
                    g.dma_start(rwbd[64:128, c, 64:128],
                                rw_h[2 * c + 1, :, :]).then_inc(sems["wt"], 16)
                for sb, nm in ((bqf, "bq"), (bkf, "bk"), (murf, "mur"),
                               (mukf, "muk")):
                    g.dma_start(sb[:, :],
                                b_h[nm].rearrange("(c p) -> p c", p=128),
                                allow_slow_non_contiguous=True,
                                ).then_inc(sems["wt"], 16)
                for sl, nm in ((borow, "bo"), (bvrow, "bv")):
                    g.dma_start(sl, b_h[nm].rearrange("(o e) -> o e", o=1)
                                ).then_inc(sems["wt"], 16)
                g.memset(lneps[:, :], LNEPS)
                g.memset(fence[:, :], 0.0)
                g.memset(ones128[:, :], 1.0)
                g.memset(idn[:, :], 0.0)
                g.drain()
                g.affine_select(out=idn[:, :], in_=idn[:, :],
                                compare_op=OP.not_equal, fill=1.0, base=0,
                                pattern=[[-1, 128]], channel_multiplier=1)
                g.memset(A2[:, :, :, :], 1.0)
                g.memset(kvm[:, :, :], 0.0)
                g.memset(ksumM[:, :], 0.0)
                return g.memset(h2fm[:, :, :], 0.0)
            t_g = G("g", body)
            cnt["wt"] += 16 * N_SETUP_DMA
            return [t_g, ("wt", cnt["wt"])]

        t_setup = setup()
        Graw("s", lambda sc: sc.preload_act_table(AF.Sqrt))

        xsv = x_h.rearrange("(n p) d -> p n d", p=128)
        osv = out_h.rearrange("(n p) d -> p n d", p=128)

        WUSE = ["wk"]
        for i in range(NT):
            WUSE.append("wv")
            if i + 1 < NT:
                WUSE.append("wk")
        WUSE.append("wq")
        for i in range(NT):
            WUSE.append("wo")
            if i + 1 < NT:
                WUSE.append("wq")
            WUSE += ["wr", "wkf", "wvf"]

        w_load_i = [0]
        w_use_i = [0]
        w_free_tok = {}

        def w_load(nmax):
            while w_load_i[0] < min(nmax, len(WUSE)):
                u = w_load_i[0]
                if u >= 3 and (u - 3) not in w_free_tok:
                    break
                w_load_i[0] += 1
                slot = u % 3
                W("sy", w_free_tok[u - 3] if u >= 3 else None)
                if u < 3:
                    Wm("sy", t_setup)
                nm = WUSE[u]
                def body(sy, nm=nm, slot=slot):
                    return sy.dma_start(
                        wpool[:, slot, :, :],
                        wbf_h[nm].rearrange("(c p) e -> p c e", p=128))
                G("sy", body, sem="wl%d" % slot, n=16)

        def w_use():
            u = w_use_i[0]
            w_use_i[0] += 1
            slot = u % 3
            t = ("wl%d" % slot, 16 * (u // 3 + 1))
            w_load(u + 5)
            return wpool[:, slot, :, :], t, u

        t_x_f = {}
        t_x_b = {}
        t_xfree_f = {}
        t_xfree_b = {}

        def load_tile_f32(j, i):
            for s in range(NSUB):
                g_idx = j * NSUB + s
                slot = g_idx % NXS
                Wm("sy", t_xfree_f.get(g_idx - NXS, []))
                if g_idx == 0:
                    Wm("sy", t_setup)
                def b1(sy, src_sub=i * NSUB + s, slot=slot):
                    return sy.dma_start(xA[:, slot, :], xsv[:, src_sub, :])
                t_x_f[g_idx] = G("sy", b1, sem="ld%d" % slot, n=16)

        def load_tile_bf(j, i):
            for s in range(NSUB):
                g_idx = j * NSUB + s
                slot = g_idx % NXS
                Wm("g", t_xfree_b.get(g_idx - NXS, []))
                def b2(g, src_sub=i * NSUB + s, slot=slot):
                    return g.dma_start(xbf[:, slot, :], xsv[:, src_sub, :])
                t_x_b[g_idx] = G("g", b2, sem="xb%d" % slot, n=16)

        def load_tile(j, i):
            load_tile_f32(j, i)
            load_tile_bf(j, i)

        t_pkfree = {0: [], 1: []}
        t_trpfree = [None, None]
        trp_i = [0]
        t_outfree = {}
        t_ltr_prev = [None]
        h2_pe_prev = [None]
        t_h1fm_readers = []
        t_bigfree = {0: [], 1: []}
        t_mix_prev = [[]]

        def trp_next(passb):
            if not passb:
                return 0
            i_ = trp_i[0]
            trp_i[0] = 1 - i_
            return i_

        def ln_chain(j, i):
            st = stX
            xread = []
            for s in range(NSUB):
                W("s", t_x_f[j * NSUB + s])
            W("s", tok("v"))
            def b_sq(sc, j=j):
                for s in range(NSUB):
                    sc.activation(junk[:, 0:1024],
                                  xA[:, (j * NSUB + s) % NXS, :],
                                  AF.Square, accum_out=st(SC_SQ, s))
                    sc.drain()
                return sc.mul(fence[0:1, 0:1], fence[0:1, 0:1], 1.0)
            t_sq = G("s", b_sq)
            xread.append(t_sq)
            for s in range(NSUB):
                W("v", t_x_f[j * NSUB + s])
            W("v", t_sq)
            def b_red(v, j=j):
                for s in range(NSUB):
                    v.reduce_sum(st(SC_SUM, s), xA[:, (j * NSUB + s) % NXS, :],
                                 axis=AXX)
                return v.drain()
            t_red = G("v", b_red)
            xread.append(t_red)

            def b_chain(v):
                v.tensor_scalar(st(SC_M), st(SC_SUM), 1.0 / D, None, OP.mult)
                v.drain()
                v.tensor_tensor(st(SC_T), st(SC_M), st(SC_M), OP.mult)
                v.drain()
                v.scalar_tensor_tensor(st(SC_VAR), st(SC_SQ), 1.0 / D,
                                       st(SC_T), OP.mult, OP.subtract)
                return v.drain()
            t_var = G("v", b_chain)
            W("s", t_var)
            t_std = G("s", lambda sc: sc.activation(st(SC_STD), st(SC_VAR),
                                                    AF.Sqrt,
                                                    bias=lneps[:, 0:1]))
            W("v", t_std)
            def b_rstd(v, i=i):
                v.reciprocal(stall[:, i, :, 0], st(SC_STD))
                v.drain()
                v.scalar_tensor_tensor(stall[:, i, :, 1], st(SC_M), -1.0,
                                       stall[:, i, :, 0], OP.mult, OP.mult)
                return v.drain()
            t_rst = G("v", b_rstd)
            for s in range(NSUB):
                t_xfree_f[j * NSUB + s] = xread[:]
            return t_rst

        def mk_diag(i, t_ready):
            par = i % 2
            W("s", t_ready)
            W("s", t_ltr_prev[0])
            def b_diag(sc, i=i, par=par):
                for s in range(NSUB):
                    sc.activation(diag2[:, par, s, :], idn[:, :], AF.Identity,
                                  scale=stall[:, i, s, 0:1])
                    sc.activation(diagn[:, par, s, :], idn[:, :], AF.Identity,
                                  scale=stall[:, i, s, 1:2])
                sc.drain()
                return sc.mul(fence[0:1, 0:1], fence[0:1, 0:1], 1.0)
            return G("s", b_diag)

        def ln_tr(j, i, t_diag, passb):
            par = i % 2
            t_h1fm = []
            for s in range(NSUB):
                g_idx = j * NSUB + s
                for half in range(2):
                    ti = trp_next(passb)
                    W("pe", t_diag)
                    W("pe", t_x_b[g_idx])
                    W("pe", t_trpfree[ti])
                    Wm("pe", t_h1fm_readers)
                    def b_tr(pe, s=s, ti=ti, par=par, half=half, g_idx=g_idx):
                        for cc in range(4):
                            c = half * 4 + cc
                            pe.matmul(trF32[ti][:, cc * 128:(cc + 1) * 128],
                                      xbf[:, g_idx % NXS,
                                          c * 128:(c + 1) * 128],
                                      diag2[:, par, s, :],
                                      start=(cc == 0), stop=False,
                                      skip_group_check=True)
                        for cc in range(4):
                            ins = pe.matmul(
                                trF32[ti][:, cc * 128:(cc + 1) * 128],
                                ones128[:, :], diagn[:, par, s, :],
                                start=False, stop=True, skip_group_check=True)
                        return ins
                    t_tr = G("pe", b_tr)
                    t_ltr_prev[0] = t_tr
                    t_xfree_b[g_idx] = t_xfree_b.get(g_idx, []) + [t_tr]
                    W("s", t_tr)
                    def b_ev(sc, s=s, ti=ti, half=half):
                        return sc.activation(
                            h1fm[:, half * 4:(half + 1) * 4,
                                 s * 128:(s + 1) * 128],
                            trF32[ti].rearrange("p (c t) -> p c t", c=4),
                            AF.Copy)
                    t_ev = G("s", b_ev)
                    t_trpfree[ti] = t_ev
                    t_h1fm.append(t_ev)
            del t_h1fm_readers[:]
            return t_h1fm

        def proj_fm(rhs_fn, evac_fn, extra_wait=()):
            wap, t_w, u = w_use()
            last_mm = [None]
            for c in range(NCH):
                bank = c % 2
                Wm("pe", t_pkfree[bank])
                Wm("pe", list(extra_wait))
                W("pe", t_w)
                def b_mm(pe, c=c, bank=bank, wap=wap):
                    for kd in range(NCH):
                        ins = pe.matmul(pk[bank][:, 0:512],
                                        wap[:, kd, c * 128:(c + 1) * 128],
                                        rhs_fn(kd),
                                        start=(kd == 0), stop=(kd == NCH - 1))
                    return ins
                t_mm = G("pe", b_mm)
                last_mm[0] = t_mm
                t_pkfree[bank] = evac_fn(c, bank, t_mm)
            w_free_tok[u] = last_mm[0]

        def pass_a_s1(j, i, t_diag):
            t_h1fm = ln_tr(j, i, t_diag, passb=False)
            def ev_k(c, bank, t_mm):
                W("v", t_mm)
                def b(v, c=c, bank=bank):
                    return v.tensor_scalar(kqbf[:, c, :], pk[bank][:, 0:512],
                                           bkf[:, c:c + 1], 0.0, OP.add, OP.add)
                return [G("v", b)]
            proj_fm(lambda kd: h1fm[:, kd, :], ev_k, extra_wait=t_h1fm)
            return tok("v"), t_h1fm

        def pass_a_kp_v(j, i, t_kbf, t_h1fm):
            deps = {}
            for s in range(NSUB):
                t_kp = []
                for half in range(2):
                    W("pe", t_kbf)
                    def b_kp(pe, s=s, half=half):
                        for cc in range(4):
                            c = half * 4 + cc
                            ins = pe.matmul(kpP[:, cc * 128:(cc + 1) * 128],
                                            kqbf[:, c, s * 128:(s + 1) * 128],
                                            rwbd[:, c, :], start=True,
                                            stop=True, skip_group_check=True)
                        return ins
                    t_mm = G("pe", b_kp)
                    W("v", t_mm)
                    def b_kpev(v, s=s, half=half):
                        return v.tensor_scalar(
                            A1[:, s, half * 512:(half + 1) * 512],
                            kpP[:, :], 0.0, EPS, OP.max, OP.add)
                    t_ev = G("v", b_kpev)
                    t_kp.append(t_ev)
                    W("pe", t_ev)
                deps.setdefault(s, {})["kp"] = t_kp

            wap, t_w, u = w_use()
            last_mm = [None]
            for s in range(NSUB):
                t_v = []
                for half in range(2):
                    W("pe", t_w)
                    Wm("pe", t_h1fm)
                    def b_v(pe, s=s, half=half, wap=wap):
                        for kd in range(NCH):
                            pe.matmul(vP[:, :],
                                      h1fm[:, kd, s * 128:(s + 1) * 128],
                                      wap[:, kd, half * 512:(half + 1) * 512],
                                      start=(kd == 0), stop=False)
                        return pe.matmul(vP[:, :], ones128[32:33, :],
                                         bvrow[:, half * 512:(half + 1) * 512],
                                         start=False, stop=True)
                    t_mm = G("pe", b_v)
                    last_mm[0] = t_mm
                    t_h1fm_readers.append(t_mm)
                    W("v", t_mm)
                    def b_vev(v, s=s, half=half):
                        return v.tensor_copy(
                            A2[:, s, half * 4:(half + 1) * 4, 1:129],
                            vP[:, :].rearrange("p (c t) -> p c t", c=4))
                    t_ev = G("v", b_vev)
                    t_v.append(t_ev)
                    W("pe", t_ev)
                deps[s]["v"] = t_v
            w_free_tok[u] = last_mm[0]
            return deps

        def pass_a_kv(j, i, deps):
            for s in range(NSUB):
                Wm("pe", deps[s]["kp"] + deps[s]["v"])
                def b_kv(pe, s=s, i=i):
                    for c in range(NCH):
                        ins = pe.matmul(
                            kvP[:, KVCOL[c]:KVCOL[c] + 129],
                            A1[:, s, c * 128:(c + 1) * 128],
                            A2[:, s, c, :],
                            start=(i == 0 and s == 0 and c in KVSTART),
                            stop=(i == NT - 1 and s == NSUB - 1),
                            skip_group_check=True)
                    return ins
                G("pe", b_kv)

        def finalize_a():
            t_kv = tok("pe")
            W("v", t_kv)
            def b_fin(v):
                for c in range(NCH):
                    col = KVCOL[c]
                    v.tensor_copy(kvm[0:64, c, 0:64], kvP[0:64, col + 1:col + 65])
                    v.tensor_copy(kvm[64:128, c, 64:128],
                                  kvP[64:128, col + 65:col + 129])
                    v.tensor_copy(ksumM[0:64, 2 * c:2 * c + 1],
                                  kvP[0:64, col:col + 1])
                    ins = v.tensor_copy(ksumM[64:128, 2 * c + 1:2 * c + 2],
                                        kvP[64:128, col:col + 1])
                return ins
            return G("v", b_fin)

        def pass_b_s1(j, i, t_diag):
            t_h1fm = ln_tr(j, i, t_diag, passb=True)
            def ev_q(c, bank, t_mm):
                W("s", t_mm)
                def b(sc, c=c, bank=bank):
                    return sc.activation(kqbf[:, c, :], pk[bank][:, 0:512],
                                         AF.Identity, bias=bqf[:, c:c + 1])
                return [G("s", b)]
            proj_fm(lambda kd: h1fm[:, kd, :], ev_q, extra_wait=t_h1fm)

            for c in range(NCH):
                bank = c % 2
                Wm("pe", t_pkfree[bank])
                def b_qp(pe, c=c, bank=bank):
                    return pe.matmul(pk[bank][:, 0:512], rwbd[:, c, :],
                                     kqbf[:, c, :], start=True, stop=True)
                t_mm = G("pe", b_qp)
                W("v", t_mm)
                def b_qpev(v, c=c, bank=bank):
                    return v.tensor_scalar(qpf[:, c, :], pk[bank][:, 0:512],
                                           0.0, EPS, OP.max, OP.add)
                t_pkfree[bank] = [G("v", b_qpev)]
            return tok("v")

        def chain_rstd(st, t_ready, diags=None, wait2=None):
            W("v", t_ready)
            def b_chain(v):
                v.tensor_scalar(st(SC_M), st(SC_SUM), -1.0 / D, None, OP.mult)
                v.drain()
                v.tensor_tensor(st(SC_T), st(SC_M), st(SC_M), OP.mult)
                v.drain()
                v.scalar_tensor_tensor(st(SC_VAR), st(SC_SQ), 1.0 / D,
                                       st(SC_T), OP.mult, OP.subtract)
                return v.drain()
            t_var = G("v", b_chain)
            W("s", t_var)
            t_std = G("s", lambda sc: sc.activation(st(SC_STD), st(SC_VAR),
                                                    AF.Sqrt,
                                                    bias=lneps[:, 0:1]))
            W("v", t_std)
            if wait2:
                Wm("v", wait2)
            def b_rstd(v):
                v.reciprocal(st(SC_RSTD), st(SC_STD))
                v.drain()
                v.scalar_tensor_tensor(st(SC_NEGM), st(SC_M), 1.0,
                                       st(SC_RSTD), OP.mult, OP.mult)
                ins = v.drain()
                if diags is not None:
                    d2ap, dnap = diags
                    v.tensor_scalar(d2ap, idn[:, :], st(SC_RSTD), None,
                                    OP.mult)
                    v.tensor_scalar(dnap, idn[:, :], st(SC_NEGM), None,
                                    OP.mult)
                    ins = v.drain()
                return ins
            return G("v", b_rstd)

        def pass_b_attn(j, i, t_qp, t_kvm_tok, wo, mid_fn=None):
            t_div = {}
            t_x2tok = {}
            t_h2fm = []
            t_ev_sub = {}
            t_xr_s = {}
            t_xk_s = {}

            def attn_sub(s):
                W("pe", t_kvm_tok)
                W("pe", t_qp)
                if s >= 1:
                    W("pe", t_div[s - 1])
                else:
                    Wm("pe", t_pkfree[0] + t_pkfree[1])
                ti_n = trp_next(True)
                W("pe", t_trpfree[ti_n])
                def b_qkv(pe, s=s, ti=ti_n):
                    for c in range(NCH):
                        pe.matmul(pkq[:, c * 128:(c + 1) * 128],
                                  qpf[:, c, s * 128:(s + 1) * 128],
                                  kvm[:, c, :], start=True, stop=True,
                                  skip_group_check=True)
                    for c in range(NCH):
                        ins = pe.matmul(trF32[ti][:, 2 * c:2 * c + 2],
                                        qpf[:, c, s * 128:(s + 1) * 128],
                                        ksumM[:, 2 * c:2 * c + 2],
                                        start=True, stop=True,
                                        skip_group_check=True)
                    return ins
                t_qkv = G("pe", b_qkv)

                W("v", t_qkv)
                def b_recip(v, s=s, ti=ti_n):
                    v.tensor_scalar(recip[:, s, :], trF32[ti][:, 0:16],
                                    EPS, None, OP.add)
                    v.drain()
                    v.reciprocal(recip[:, s, :], recip[:, s, :])
                    return v.drain()
                t_rc = G("v", b_recip)
                t_trpfree[ti_n] = t_rc

                def b_div(v, s=s):
                    return v.tensor_tensor(
                        A1[:, s, :].rearrange("p (h u) -> p h u", h=16),
                        pkq.rearrange("p (h u) -> p h u", h=16),
                        recip[:, s, :].unsqueeze(2).broadcast_to([128, 16, 64]),
                        OP.mult)
                t_div[s] = G("v", b_div)

                ti = trp_next(True)
                W("pe", t_div[s])
                W("pe", t_trpfree[ti])
                def b_otr(pe, s=s, ti=ti):
                    for c in range(NCH):
                        ins = pe.transpose(trBF[ti][:, c * 128:(c + 1) * 128],
                                           A1[:, s, c * 128:(c + 1) * 128],
                                           idn[:, :])
                    return ins
                t_tr = G("pe", b_otr)
                W("v", t_tr)
                def b_oev(v, ti=ti):
                    return v.tensor_copy(
                        odivfm[:, :, :],
                        trBF[ti].rearrange("p (c t) -> p c t", c=NCH))
                t_oev = G("v", b_oev)
                t_trpfree[ti] = t_oev

                sl = s % 2
                W("pe", t_oev)
                Wm("pe", t_bigfree[sl])
                def b_wo(pe, sl=sl, wo=wo):
                    for c in range(NCH):
                        for half in range(2):
                            pe.matmul(bigP(sl)[:, half * 512:(half + 1) * 512],
                                      odivfm[:, c, :],
                                      wo[:, c, half * 512:(half + 1) * 512],
                                      start=(c == 0), stop=False,
                                      skip_group_check=True)
                    for half in range(2):
                        ins = pe.matmul(bigP(sl)[:, half * 512:(half + 1) * 512],
                                        ones128[0:1, :],
                                        borow[:, half * 512:(half + 1) * 512],
                                        start=False, stop=True,
                                        skip_group_check=True)
                    return ins
                return G("pe", b_wo)

            def chain_sub(s, t_wo):
                sl = s % 2
                g_idx = j * NSUB + s
                W("s", tok("v"))
                W("s", t_wo)
                def b_asq(sc, sl=sl):
                    sc.activation(junk[:, 0:1024], bigP(sl), AF.Copy,
                                  accum_out=stA(SC_SUM, 0))
                    sc.drain()
                    sc.activation(junk[:, 0:1024], bigP(sl), AF.Square,
                                  accum_out=stA(SC_SQ, 0))
                    sc.drain()
                    return sc.mul(fence[0:1, 0:1], fence[0:1, 0:1], 1.0)
                t_asq = G("s", b_asq)
                chain_rstd(stA, t_asq)

                W("v", t_x_f[g_idx])
                def b_x2(v, s=s, sl=sl, g_idx=g_idx):
                    v.tensor_scalar(x2[:, s, :], xA[:, g_idx % NXS, :],
                                    stA(SC_NEGM, 0), 0.0, OP.add, OP.add)
                    v.drain()
                    v.scalar_tensor_tensor(x2[:, s, :], bigP(sl),
                                           stA(SC_RSTD, 0), x2[:, s, :],
                                           OP.mult, OP.add,
                                           accum_out=stB(SC_SUM, 0))
                    return v.drain()
                t_o = G("v", b_x2)
                t_xfree_f[g_idx] = t_xfree_f.get(g_idx, []) + [t_o]
                t_bigfree[sl] = [t_o]
                t_x2tok[s] = t_o

                def b_x2sq(v, s=s):
                    v.scalar_tensor_tensor(vjunk[:, :], x2[:, s, :], 1.0,
                                           x2[:, s, :], OP.mult, OP.mult,
                                           accum_out=stB(SC_SQ, 0))
                    return v.drain()
                t_x2sq = G("v", b_x2sq)
                t_h2d = chain_rstd(
                    stB, t_x2sq,
                    diags=(h2d2[:, s % 2, :], h2dn[:, s % 2, :]),
                    wait2=[h2_pe_prev[0]])

                W("v", h2_pe_prev[0])
                def b_cast(v, s=s):
                    return v.tensor_copy(x2bfb[:, s % 2, :], x2[:, s, :])
                t_cast = G("v", b_cast)
                for half in range(2):
                    ti = trp_next(True)
                    W("pe", t_h2d)
                    W("pe", t_cast)
                    W("pe", t_trpfree[ti])
                    def b_h2mm(pe, s=s, ti=ti, half=half):
                        for cc in range(4):
                            c = half * 4 + cc
                            pe.matmul(trF32[ti][:, cc * 128:(cc + 1) * 128],
                                      x2bfb[:, s % 2, c * 128:(c + 1) * 128],
                                      h2d2[:, s % 2, :],
                                      start=(cc == 0), stop=False,
                                      skip_group_check=True)
                        for cc in range(4):
                            ins = pe.matmul(
                                trF32[ti][:, cc * 128:(cc + 1) * 128],
                                ones128[:, :], h2dn[:, s % 2, :],
                                start=False, stop=True, skip_group_check=True)
                        return ins
                    t_tr = G("pe", b_h2mm)
                    h2_pe_prev[0] = t_tr
                    W("s", t_tr)
                    Wm("s", t_mix_prev[0])
                    def b_h2ev(sc, s=s, ti=ti, half=half):
                        return sc.activation(
                            h2fm[:, half * 4:(half + 1) * 4,
                                 1 + s * 128:1 + (s + 1) * 128],
                            trF32[ti].rearrange("p (c t) -> p c t", c=4),
                            AF.Copy)
                    t_ev = G("s", b_h2ev)
                    t_trpfree[ti] = t_ev
                    t_h2fm.append(t_ev)
                    t_ev_sub.setdefault(s, []).append(t_ev)

                Wm("v", t_ev_sub[s])
                if s > 0:
                    Wm("v", t_ev_sub[s - 1])
                Wm("v", t_mix_prev[0])
                Wm("v", t_ffn_free[0])
                lo, hi = s * 128, (s + 1) * 128
                def b_ds(v, s=s, lo=lo, hi=hi):
                    v.tensor_tensor(dfm[:, :, lo:hi],
                                    h2fm[:, :, 1 + lo:1 + hi],
                                    h2fm[:, :, lo:hi], OP.subtract)
                    v.drain()
                    for c in range(NCH):
                        v.scalar_tensor_tensor(xrb[:, c, lo:hi],
                                               dfm[:, c, lo:hi],
                                               murf[:, c:c + 1],
                                               h2fm[:, c, lo:hi],
                                               OP.mult, OP.add)
                    return v.drain()
                t_xr_s[s] = G("v", b_ds)
                W("g", t_xr_s[s])
                Wm("g", t_ffn_free[0])
                def b_xks(g, s=s, lo=lo, hi=hi):
                    g.tensor_tensor(xkv3[:, :, lo:hi], dfm[:, :, lo:hi],
                                    mukf[:, :].unsqueeze(2).broadcast_to(
                                        [128, NCH, 128]), OP.mult)
                    g.drain()
                    return g.tensor_tensor(xkv3[:, :, lo:hi],
                                           xkv3[:, :, lo:hi],
                                           h2fm[:, :, lo:hi], OP.add)
                t_xk_s[s] = G("g", b_xks)

            t_wo = {}
            t_wo[0] = attn_sub(0)
            t_wo[1] = attn_sub(1)
            chain_sub(0, t_wo[0])
            t_wo[2] = attn_sub(2)
            chain_sub(1, t_wo[1])
            t_wo[3] = attn_sub(3)
            chain_sub(2, t_wo[2])
            t_pkfree[0] = [t_div[3]]
            t_pkfree[1] = [t_div[3]]
            mid_ret = mid_fn() if mid_fn is not None else None
            chain_sub(3, t_wo[3])
            Wm("v", [t_xk_s[3]])
            t_mix = [G("v", lambda v: v.tensor_copy(h2fm[:, :, 0:1],
                                                    h2fm[:, :, TT:TT + 1]))]
            return t_h2fm, t_xr_s[3], t_xk_s[3], t_mix, mid_ret

        xkv = A2[:, :, :, :].rearrange("p a b c -> p (a b c)")
        xkv3 = xkv[:, 0:NCH * TT].rearrange("p (c t) -> p c t", c=NCH)
        t_ffn_free = [[]]

        def pass_b_tail(j, i, t_xr, t_xk, t_mix):
            st = stF

            first_r = [True]
            t_v_mark = tok("v")
            def ev_r(c, bank, t_mm):
                W("s", t_mm)
                W("s", t_v_mark)
                def b(sc, c=c, bank=bank):
                    if first_r[0]:
                        sc.preload_act_table(AF.Sigmoid)
                        first_r[0] = False
                    return sc.activation(rbf[:, c, :], pk[bank][:, 0:512],
                                         AF.Sigmoid)
                return [G("s", b)]
            proj_fm(lambda kd: xrb[:, kd, :], ev_r, extra_wait=[t_xr])
            Graw("s", lambda sc: sc.preload_act_table(AF.Sqrt))
            t_last_relu = []

            t_krelu = []
            t_g_mark = tok("g")
            def ev_k2(c, bank, t_mm):
                W("s", t_mm)
                W("s", t_g_mark)
                def b(sc, c=c, bank=bank):
                    return sc.activation(xrb[:, c, :], pk[bank][:, 0:512],
                                         AF.Relu)
                t = G("s", b)
                t_krelu.append(t)
                t_last_relu[:] = [t]
                return [t]
            proj_fm(lambda kd: xkv[:, kd * TT:(kd + 1) * TT], ev_k2,
                    extra_wait=[t_xk])
            Wm("g", t_krelu)
            W("g", tok("v"))
            def b_sqr(g):
                for c in range(NCH):
                    ins = g.tensor_tensor(dfm[:, c, :], xrb[:, c, :],
                                          xrb[:, c, :], OP.mult)
                return ins
            t_kact = G("g", b_sqr)

            def ev_f(c, bank, t_mm):
                W("v", t_mm)
                def b(v, c=c, bank=bank):
                    return v.tensor_tensor(ffm[:, c, :], pk[bank][:, 0:512],
                                           rbf[:, c, :], OP.mult)
                return [G("v", b)]
            proj_fm(lambda kd: dfm[:, kd, :], ev_f, extra_wait=[t_kact])
            t_f = tok("v")
            t_ffn_free[0] = [tok("pe"), t_kact] + t_last_relu

            for p in range(2):
                t_fsq = None
                for q_ in range(2):
                    s = p * 2 + q_
                    ti = trp_next(True)
                    W("pe", t_f)
                    W("pe", t_trpfree[ti])
                    def b_ftr(pe, s=s, ti=ti):
                        for c in range(NCH):
                            ins = pe.transpose(
                                trBF[ti][:, c * 128:(c + 1) * 128],
                                ffm[:, c, s * 128:(s + 1) * 128], idn[:, :])
                        return ins
                    t_tr = G("pe", b_ftr)
                    W("s", t_tr)
                    W("s", tok("v"))
                    def b_fev(sc, s=s, ti=ti, q_=q_):
                        return sc.activation(A1[:, s, :], trBF[ti][:, :],
                                             AF.Copy, accum_out=st(SC_SUM, q_))
                    t_ev = G("s", b_fev)
                    t_trpfree[ti] = t_ev
                    def b_fsq(sc, s=s, q_=q_):
                        sc.drain()
                        sc.activation(junk[:, 0:1024], A1[:, s, :], AF.Square,
                                      accum_out=st(SC_SQ, q_))
                        sc.drain()
                        return sc.mul(fence[0:1, 0:1], fence[0:1, 0:1], 1.0)
                    t_fsq = G("s", b_fsq)
                chain_rstd(st, t_fsq)

                for q_ in range(2):
                    s = p * 2 + q_
                    sidx = i * NSUB + s
                    slot = sidx % 2
                    W("v", t_outfree.get(sidx - 2))
                    def b_out(v, s=s, q_=q_, slot=slot):
                        v.tensor_scalar(outS[:, slot, :], x2[:, s, :],
                                        st(SC_NEGM, q_), 0.0, OP.add, OP.add)
                        v.drain()
                        return v.scalar_tensor_tensor(
                            outS[:, slot, :], A1[:, s, :],
                            st(SC_RSTD, q_), outS[:, slot, :],
                            OP.mult, OP.add)
                    t_o = G("v", b_out)
                    W("g", t_o)
                    def b_st(g, sidx=sidx, slot=slot):
                        return g.dma_start(osv[:, sidx, :], outS[:, slot, :])
                    t_outfree[sidx] = G("g", b_st, sem="st%d" % slot, n=16)
            return t_mix

        for E in ("pe", "v", "s"):
            Wm(E, t_setup)

        w_load(3)
        load_tile(0, 0)
        t_rst = ln_chain(0, 0)
        t_dg = mk_diag(0, t_rst)
        t_kbf, t_h1fm = pass_a_s1(0, 0, t_dg)
        for i in range(NT):
            if i + 1 < NT:
                load_tile(i + 1, i + 1)
                t_rst = ln_chain(i + 1, i + 1)
                t_dg = mk_diag(i + 1, t_rst)
            if i == 0:
                dump("h1fmA", h1fm[:, :, :].rearrange("p a b -> p (a b)"),
                     tok("s"))
                dump("kbf", kqbf[:, :, :].rearrange("p a b -> p (a b)"),
                     tok("v"))
            deps = pass_a_kp_v(i, i, t_kbf, t_h1fm)
            if i == 0:
                dump("kp", A1[:, :, :].rearrange("p a b -> p (a b)"), tok("v"))
                dump("vext", A2[:, :, :, :].rearrange("p a b c -> p (a b c)"),
                     tok("v"))
            pass_a_kv(i, i, deps)
            if i + 1 < NT:
                t_kbf, t_h1fm = pass_a_s1(i + 1, i + 1, t_dg)

        t_kvm_tok = finalize_a()
        dump("kvm", kvm[:, :, :].rearrange("p a b -> p (a b)"), t_kvm_tok)
        dump("ksumM", ksumM[:, :], t_kvm_tok)
        t_trpfree[1] = t_kvm_tok

        load_tile(NT, 0)
        t_dg = mk_diag(0, t_kvm_tok)
        t_qp = pass_b_s1(NT, 0, t_dg)
        for i in range(NT):
            if i + 1 < NT:
                load_tile_bf(NT + i + 1, i + 1)
                t_dg = mk_diag(i + 1, t_kvm_tok)
            wo, t_wo, u_wo = w_use()
            W("pe", t_wo)
            if i == 0:
                dump("qpf", qpf[:, :, :].rearrange("p a b -> p (a b)"),
                     tok("v"))
            def mid_fn(i=i, t_dg=t_dg):
                if i + 1 >= NT:
                    return None
                return pass_b_s1(NT + i + 1, i + 1, t_dg)
            t_h2fm_all, t_xr3, t_xk3, t_mix, t_qp_next = pass_b_attn(
                NT + i, i, t_qp, t_kvm_tok, wo, mid_fn)
            w_free_tok[u_wo] = tok("pe")
            if i + 1 < NT:
                load_tile_f32(NT + i + 1, i + 1)
            if i == 0:
                dump("x2", x2[:, :, :].rearrange("p a b -> p (a b)"), tok("v"))
                dump("h2fm", h2fm[:, :, :].rearrange("p a b -> p (a b)"),
                     tok("s"))
            if i + 1 < NT:
                t_qp = t_qp_next
            pass_b_tail(NT + i, i, t_xr3, t_xk3, t_mix)
            t_mix_prev[0] = t_mix
            if i == 0:
                dump("ffm", ffm[:, :, :].rearrange("p a b -> p (a b)"),
                     tok("v"))
                dump("ftm", A1[:, :, :].rearrange("p a b -> p (a b)"),
                     tok("s"))

        def g_end(g):
            g.wait_ge(sems["st0"], 16 * (NT * NSUB // 2))
            g.wait_ge(sems["st1"], 16 * (NT * NSUB // 2))
            if dbg:
                g.wait_ge(sems["wt"], cnt["wt"])
        Graw("g", g_end)

        block = e(nc.Block())

        def run(E):
            def body(eng):
                for f in q[E]:
                    f(eng)
            return body
        block.tensor(run("pe"))
        block.vector(run("v"))
        block.scalar(run("s"))
        block.gpsimd(run("g"))
        block.sync(run("sy"))

    return nc


_prog_cache = {}


def _get_prog(S, NT):
    key = (S, NT)
    if key not in _prog_cache:
        _prog_cache[key] = build_program(S, NT)
    return _prog_cache[key]


def kernel(L0_attn_x, L0_attn_Wq, L0_attn_bq, L0_attn_Wk, L0_attn_bk,
           L0_attn_Wv, L0_attn_bv, L0_attn_Wo, L0_attn_bo, L0_attn_rw,
           L0_ffn_W_r, L0_ffn_W_k, L0_ffn_W_v, L0_ffn_mu_r, L0_ffn_mu_k):
    xin = np.asarray(L0_attn_x)
    Bb, S, Dd = xin.shape
    assert Dd == D and S % TT == 0
    NT = S // TT
    nc = _get_prog(S, NT)

    com = {
        "wq": np.ascontiguousarray(L0_attn_Wq, np.float32),
        "wk": np.ascontiguousarray(L0_attn_Wk, np.float32),
        "wv": np.ascontiguousarray(L0_attn_Wv, np.float32),
        "wo": np.ascontiguousarray(L0_attn_Wo, np.float32),
        "wr": np.ascontiguousarray(np.asarray(L0_ffn_W_r, np.float32).T),
        "wkf": np.ascontiguousarray(np.asarray(L0_ffn_W_k, np.float32).T),
        "wvf": np.ascontiguousarray(np.asarray(L0_ffn_W_v, np.float32).T),
        "rw": np.ascontiguousarray(L0_attn_rw, np.float32),
        "bq": np.ascontiguousarray(L0_attn_bq, np.float32),
        "bk": np.ascontiguousarray(L0_attn_bk, np.float32),
        "bv": np.ascontiguousarray(L0_attn_bv, np.float32),
        "bo": np.ascontiguousarray(L0_attn_bo, np.float32),
        "mur": np.ascontiguousarray(L0_ffn_mu_r, np.float32),
        "muk": np.ascontiguousarray(L0_ffn_mu_k, np.float32),
    }
    in_maps = []
    for b in range(Bb):
        m = dict(com)
        m["x"] = np.ascontiguousarray(xin[b], np.float32)
        in_maps.append(m)

    res = run_bass_kernel_spmd(nc, in_maps, core_ids=list(range(Bb)))
    out = np.stack([r["out"] for r in res.results], axis=0)
    return out.astype(np.float32)


# version 16
# speedup vs baseline: 1.0080x; 1.0080x over previous
import numpy as np
from contextlib import ExitStack

import concourse.bass as bass
import concourse.mybir as mybir
from concourse.bass_utils import run_bass_kernel_spmd

F32 = mybir.dt.float32
BF16 = mybir.dt.bfloat16
AF = mybir.ActivationFunctionType
OP = mybir.AluOpType
AXX = mybir.AxisListType.X

D = 1024
NCH = 8
H2 = 8
EPS = 1e-6
LNEPS = 1e-5
TT = 512
NSUB = 4
NXS = 4

ENGS = ("pe", "v", "s", "g", "sy")
WNAMES = ("wq", "wk", "wv", "wo", "wr", "wkf", "wvf")


def build_program(S, NT, dbg=False):
    nc = bass.Bass("TRN2", target_bir_lowering=False)

    x_h = nc.dram_tensor("x", [S, D], F32, kind="ExternalInput").ap()
    out_h = nc.dram_tensor("out", [S, D], F32, kind="ExternalOutput").ap()
    w_h = {nm: nc.dram_tensor(nm, [D, D], F32, kind="ExternalInput").ap()
           for nm in WNAMES}
    wbf_h = {nm: nc.dram_tensor("bf_" + nm, [D, D], BF16).ap()
             for nm in WNAMES}
    rw_h = nc.dram_tensor("rw", [16, 64, 64], F32, kind="ExternalInput").ap()
    b_h = {nm: nc.dram_tensor(nm, [D], F32, kind="ExternalInput").ap()
           for nm in ("bq", "bk", "bv", "bo", "mur", "muk")}

    with ExitStack() as ctx:
        e = ctx.enter_context
        semnames = ["pe", "v", "s", "g", "wt", "st0", "st1",
                    "ld0", "ld1", "ld2", "ld3", "xb0", "xb1", "xb2", "xb3",
                    "wl0", "wl1", "wl2", "wl3", "su"]
        sems = {nm: e(nc.semaphore(nm)) for nm in semnames}

        q = {E: [] for E in ENGS}
        cnt = {nm: 0 for nm in sems}
        waited = {E: {} for E in ENGS}
        OWN = {"pe": "pe", "v": "v", "s": "s", "g": "g"}

        def W(E, tok_):
            if tok_ is None:
                return
            sem, v = tok_
            if v <= 0 or OWN.get(sem) == E:
                return
            if waited[E].get(sem, 0) >= v:
                return
            waited[E][sem] = v
            q[E].append(lambda eng, s=sem, vv=v: eng.wait_ge(sems[s], vv))

        def Wm(E, toks):
            for t in toks:
                W(E, t)

        def G(E, body, sem=None, n=1):
            sname = sem or E
            def cl(eng, body=body, sname=sname, n=n):
                ins = body(eng)
                if ins is not None:
                    ins.then_inc(sems[sname], n)
            q[E].append(cl)
            cnt[sname] += n
            return (sname, cnt[sname])

        def Graw(E, body):
            q[E].append(body)

        def tok(sname):
            return (sname, cnt[sname])

        dbg_n = [0]

        def dump(name, ap, t):
            if not dbg:
                return
            hd = nc.dram_tensor("dbg_" + name, [128, ap.shape[-1]],
                                ap.dtype, kind="ExternalOutput").ap()
            W("g", t)
            def b_dump(g, hd=hd, ap=ap):
                return g.dma_start(hd, ap)
            t_d = G("g", b_dump, sem="wt", n=16)
            dbg_n[0] += 1
            for E in ("pe", "v", "s", "sy"):
                W(E, t_d)

        rwbd = e(nc.sbuf_tensor("rwbd", [128, H2, 128], BF16))
        bqf = e(nc.sbuf_tensor("bqf", [128, NCH], F32))
        bkf = e(nc.sbuf_tensor("bkf", [128, NCH], F32))
        murf = e(nc.sbuf_tensor("murf", [128, NCH], F32))
        mukf = e(nc.sbuf_tensor("mukf", [128, NCH], F32))
        idn = e(nc.sbuf_tensor("idn", [128, 128], BF16))
        ones128 = e(nc.sbuf_tensor("ones128", [128, 128], BF16))
        kvm = e(nc.sbuf_tensor("kvm", [128, H2, 128], BF16))
        ksumM = e(nc.sbuf_tensor("ksumM", [128, 16], BF16))
        lneps = e(nc.sbuf_tensor("lneps", [128, 1], F32))
        fence = e(nc.sbuf_tensor("fence", [1, 1], F32))
        stats = e(nc.sbuf_tensor("stats", [128, 64], F32))
        stall = e(nc.sbuf_tensor("stall", [128, NT, NSUB, 2], F32))
        recip = e(nc.sbuf_tensor("recip", [128, NSUB, 16], F32))
        junk = e(nc.sbuf_tensor("junk", [128, 2048], BF16))
        borow = junk[0:1, 1024:2048]
        bvrow = junk[32:33, 1024:2048]
        diag2 = e(nc.sbuf_tensor("diag2", [128, 2, NSUB, 128], BF16))
        diagn = e(nc.sbuf_tensor("diagn", [128, 2, NSUB, 128], BF16))
        h2d2 = e(nc.sbuf_tensor("h2d2", [128, 2, 128], BF16))
        h2dn = e(nc.sbuf_tensor("h2dn", [128, 2, 128], BF16))

        wpool = e(nc.sbuf_tensor("wpool", [128, 3, NCH, D], BF16))

        xA = e(nc.sbuf_tensor("xA", [128, NXS, D], F32))
        xbf = e(nc.sbuf_tensor("xbf", [128, NXS, D], BF16))
        h1fm = e(nc.sbuf_tensor("h1fm", [128, NCH, TT], BF16))
        kqbf = e(nc.sbuf_tensor("kqbf", [128, NCH, TT], BF16))
        qpf = e(nc.sbuf_tensor("qpf", [128, NCH, TT], BF16))
        A1 = e(nc.sbuf_tensor("A1", [128, NSUB, D], BF16))
        A2 = e(nc.sbuf_tensor("A2", [128, NSUB, H2, 129], BF16))
        dfm = e(nc.sbuf_tensor("dfm", [128, NCH, TT], BF16))
        xrb = e(nc.sbuf_tensor("xrb", [128, NCH, TT], BF16))
        ffm = e(nc.sbuf_tensor("ffm", [128, NCH, TT], BF16))
        h2fm = e(nc.sbuf_tensor("h2fm", [128, NCH, TT + 1], BF16))
        x2 = e(nc.sbuf_tensor("x2", [128, NSUB, D], F32))
        outS = e(nc.sbuf_tensor("outS", [128, 2, D], F32))
        AR2 = e(nc.sbuf_tensor("AR2", [128, 4096], BF16))
        vjunk = e(nc.sbuf_tensor("vjunk", [128, 1024], BF16))
        odivfm = AR2[:, 0:1024].rearrange("p (c t) -> p c t", c=NCH)
        x2bfb = AR2[:, 1024:3072].rearrange("p (a d) -> p a d", a=2)
        rbf = AR2[:, 0:4096].rearrange("p (c t) -> p c t", c=NCH)

        SC_SUM, SC_SQ, SC_M, SC_T, SC_VAR, SC_STD, SC_RSTD, SC_NEGM = range(8)

        def mkst(base, nsub):
            def st(col, s=None):
                b = base + col * nsub
                if s is None:
                    return stats[:, b:b + nsub]
                return stats[:, b + s:b + s + 1]
            return st
        stX = mkst(32, NSUB)
        stF = mkst(0, 2)
        stA = mkst(8, 1)
        stB = mkst(16, 1)

        PS = e(nc.psum_tensor("PS", [128, 4096], F32))
        trF32 = [PS[:, 0:512], PS[:, 3584:4096]]
        trBF = [PS[:, 0:512].bitcast(BF16), PS[:, 3584:4096].bitcast(BF16)]
        pk = [PS[:, 512:1024], PS[:, 1024:1536]]
        kvP = PS[:, 1536:3072]
        vP = PS[:, 3072:3584]
        kpP = PS[:, 3584:4096]
        KVCOL = [0, 129, 258, 512, 641, 770, 1024, 1153]
        KVSTART = {0, 3, 6}

        def bigP(sl):
            return PS[:, 1536 + sl * D: 1536 + (sl + 1) * D]

        pkq = PS[:, 512:1536]

        N_SETUP_DMA = 7 + 2 * H2 + 4 + 2

        def setup():
            def body(g):
                for nm in WNAMES:
                    g.dma_start(wbf_h[nm], w_h[nm]).then_inc(sems["wt"], 16)
                g.memset(rwbd[:, :, :], 0.0).then_inc(sems["su"], 1)
                g.wait_ge(sems["su"], 1)
                for c in range(H2):
                    g.dma_start(rwbd[0:64, c, 0:64],
                                rw_h[2 * c, :, :]).then_inc(sems["wt"], 16)
                    g.dma_start(rwbd[64:128, c, 64:128],
                                rw_h[2 * c + 1, :, :]).then_inc(sems["wt"], 16)
                for sb, nm in ((bqf, "bq"), (bkf, "bk"), (murf, "mur"),
                               (mukf, "muk")):
                    g.dma_start(sb[:, :],
                                b_h[nm].rearrange("(c p) -> p c", p=128),
                                allow_slow_non_contiguous=True,
                                ).then_inc(sems["wt"], 16)
                for sl, nm in ((borow, "bo"), (bvrow, "bv")):
                    g.dma_start(sl, b_h[nm].rearrange("(o e) -> o e", o=1)
                                ).then_inc(sems["wt"], 16)
                g.memset(lneps[:, :], LNEPS)
                g.memset(fence[:, :], 0.0)
                g.memset(ones128[:, :], 1.0)
                g.memset(idn[:, :], 0.0)
                g.drain()
                g.affine_select(out=idn[:, :], in_=idn[:, :],
                                compare_op=OP.not_equal, fill=1.0, base=0,
                                pattern=[[-1, 128]], channel_multiplier=1)
                g.memset(A2[:, :, :, :], 1.0)
                g.memset(kvm[:, :, :], 0.0)
                g.memset(ksumM[:, :], 0.0)
                return g.memset(h2fm[:, :, :], 0.0)
            t_g = G("g", body)
            cnt["wt"] += 16 * N_SETUP_DMA
            return [t_g, ("wt", cnt["wt"])]

        t_setup = setup()
        Graw("s", lambda sc: sc.preload_act_table(AF.Sqrt))

        xsv = x_h.rearrange("(n p) d -> p n d", p=128)
        osv = out_h.rearrange("(n p) d -> p n d", p=128)

        WUSE = ["wk"]
        for i in range(NT):
            WUSE.append("wv")
            if i + 1 < NT:
                WUSE.append("wk")
        WUSE.append("wq")
        for i in range(NT):
            WUSE.append("wo")
            if i + 1 < NT:
                WUSE.append("wq")
            WUSE += ["wr", "wkf", "wvf"]

        w_load_i = [0]
        w_use_i = [0]
        w_free_tok = {}

        def w_load(nmax):
            while w_load_i[0] < min(nmax, len(WUSE)):
                u = w_load_i[0]
                if u >= 3 and (u - 3) not in w_free_tok:
                    break
                w_load_i[0] += 1
                slot = u % 3
                W("sy", w_free_tok[u - 3] if u >= 3 else None)
                if u < 3:
                    Wm("sy", t_setup)
                nm = WUSE[u]
                def body(sy, nm=nm, slot=slot):
                    return sy.dma_start(
                        wpool[:, slot, :, :],
                        wbf_h[nm].rearrange("(c p) e -> p c e", p=128))
                G("sy", body, sem="wl%d" % slot, n=16)

        def w_use():
            u = w_use_i[0]
            w_use_i[0] += 1
            slot = u % 3
            t = ("wl%d" % slot, 16 * (u // 3 + 1))
            w_load(u + 5)
            return wpool[:, slot, :, :], t, u

        t_x_f = {}
        t_x_b = {}
        t_xfree_f = {}
        t_xfree_b = {}

        def load_tile_f32(j, i):
            for s in range(NSUB):
                g_idx = j * NSUB + s
                slot = g_idx % NXS
                Wm("sy", t_xfree_f.get(g_idx - NXS, []))
                if g_idx == 0:
                    Wm("sy", t_setup)
                def b1(sy, src_sub=i * NSUB + s, slot=slot):
                    return sy.dma_start(xA[:, slot, :], xsv[:, src_sub, :])
                t_x_f[g_idx] = G("sy", b1, sem="ld%d" % slot, n=16)

        def load_tile_bf(j, i):
            for s in range(NSUB):
                g_idx = j * NSUB + s
                slot = g_idx % NXS
                Wm("g", t_xfree_b.get(g_idx - NXS, []))
                def b2(g, src_sub=i * NSUB + s, slot=slot):
                    return g.dma_start(xbf[:, slot, :], xsv[:, src_sub, :])
                t_x_b[g_idx] = G("g", b2, sem="xb%d" % slot, n=16)

        def load_tile(j, i):
            load_tile_f32(j, i)
            load_tile_bf(j, i)

        t_pkfree = {0: [], 1: []}
        t_trpfree = [None, None]
        trp_i = [0]
        t_outfree = {}
        t_ltr_prev = [None]
        h2_pe_prev = [None]
        t_h1fm_readers = []
        t_bigfree = {0: [], 1: []}
        t_mix_prev = [[]]

        def trp_next(passb):
            if not passb:
                return 0
            i_ = trp_i[0]
            trp_i[0] = 1 - i_
            return i_

        def ln_chain(j, i):
            st = stX
            xread = []
            for s in range(NSUB):
                W("s", t_x_f[j * NSUB + s])
            W("s", tok("v"))
            def b_sq(sc, j=j):
                for s in range(NSUB):
                    sc.activation(junk[:, 0:1024],
                                  xA[:, (j * NSUB + s) % NXS, :],
                                  AF.Square, accum_out=st(SC_SQ, s))
                    sc.drain()
                return sc.mul(fence[0:1, 0:1], fence[0:1, 0:1], 1.0)
            t_sq = G("s", b_sq)
            xread.append(t_sq)
            for s in range(NSUB):
                W("v", t_x_f[j * NSUB + s])
            W("v", t_sq)
            def b_red(v, j=j):
                for s in range(NSUB):
                    v.reduce_sum(st(SC_SUM, s), xA[:, (j * NSUB + s) % NXS, :],
                                 axis=AXX)
                return v.drain()
            t_red = G("v", b_red)
            xread.append(t_red)

            def b_chain(v):
                v.tensor_scalar(st(SC_M), st(SC_SUM), 1.0 / D, None, OP.mult)
                v.drain()
                v.tensor_tensor(st(SC_T), st(SC_M), st(SC_M), OP.mult)
                v.drain()
                v.scalar_tensor_tensor(st(SC_VAR), st(SC_SQ), 1.0 / D,
                                       st(SC_T), OP.mult, OP.subtract)
                return v.drain()
            t_var = G("v", b_chain)
            W("s", t_var)
            t_std = G("s", lambda sc: sc.activation(st(SC_STD), st(SC_VAR),
                                                    AF.Sqrt,
                                                    bias=lneps[:, 0:1]))
            W("v", t_std)
            def b_rstd(v, i=i):
                v.reciprocal(stall[:, i, :, 0], st(SC_STD))
                v.drain()
                v.scalar_tensor_tensor(stall[:, i, :, 1], st(SC_M), -1.0,
                                       stall[:, i, :, 0], OP.mult, OP.mult)
                return v.drain()
            t_rst = G("v", b_rstd)
            for s in range(NSUB):
                t_xfree_f[j * NSUB + s] = xread[:]
            return t_rst

        def mk_diag(i, t_ready):
            par = i % 2
            W("s", t_ready)
            W("s", t_ltr_prev[0])
            def b_diag(sc, i=i, par=par):
                for s in range(NSUB):
                    sc.activation(diag2[:, par, s, :], idn[:, :], AF.Identity,
                                  scale=stall[:, i, s, 0:1])
                    sc.activation(diagn[:, par, s, :], idn[:, :], AF.Identity,
                                  scale=stall[:, i, s, 1:2])
                sc.drain()
                return sc.mul(fence[0:1, 0:1], fence[0:1, 0:1], 1.0)
            return G("s", b_diag)

        def ln_tr(j, i, t_diag, passb):
            par = i % 2
            t_h1fm = []
            for s in range(NSUB):
                g_idx = j * NSUB + s
                for half in range(2):
                    ti = trp_next(passb)
                    W("pe", t_diag)
                    W("pe", t_x_b[g_idx])
                    W("pe", t_trpfree[ti])
                    Wm("pe", t_h1fm_readers)
                    def b_tr(pe, s=s, ti=ti, par=par, half=half, g_idx=g_idx):
                        for cc in range(4):
                            c = half * 4 + cc
                            pe.matmul(trF32[ti][:, cc * 128:(cc + 1) * 128],
                                      xbf[:, g_idx % NXS,
                                          c * 128:(c + 1) * 128],
                                      diag2[:, par, s, :],
                                      start=(cc == 0), stop=False,
                                      skip_group_check=True)
                        for cc in range(4):
                            ins = pe.matmul(
                                trF32[ti][:, cc * 128:(cc + 1) * 128],
                                ones128[:, :], diagn[:, par, s, :],
                                start=False, stop=True, skip_group_check=True)
                        return ins
                    t_tr = G("pe", b_tr)
                    t_ltr_prev[0] = t_tr
                    t_xfree_b[g_idx] = t_xfree_b.get(g_idx, []) + [t_tr]
                    W("s", t_tr)
                    def b_ev(sc, s=s, ti=ti, half=half):
                        return sc.activation(
                            h1fm[:, half * 4:(half + 1) * 4,
                                 s * 128:(s + 1) * 128],
                            trF32[ti].rearrange("p (c t) -> p c t", c=4),
                            AF.Copy)
                    t_ev = G("s", b_ev)
                    t_trpfree[ti] = t_ev
                    t_h1fm.append(t_ev)
            del t_h1fm_readers[:]
            return t_h1fm

        def proj_fm(rhs_fn, evac_fn, extra_wait=()):
            wap, t_w, u = w_use()
            last_mm = [None]
            for c in range(NCH):
                bank = c % 2
                Wm("pe", t_pkfree[bank])
                Wm("pe", list(extra_wait))
                W("pe", t_w)
                def b_mm(pe, c=c, bank=bank, wap=wap):
                    for kd in range(NCH):
                        ins = pe.matmul(pk[bank][:, 0:512],
                                        wap[:, kd, c * 128:(c + 1) * 128],
                                        rhs_fn(kd),
                                        start=(kd == 0), stop=(kd == NCH - 1))
                    return ins
                t_mm = G("pe", b_mm)
                last_mm[0] = t_mm
                t_pkfree[bank] = evac_fn(c, bank, t_mm)
            w_free_tok[u] = last_mm[0]

        def pass_a_s1(j, i, t_diag):
            t_h1fm = ln_tr(j, i, t_diag, passb=False)
            def ev_k(c, bank, t_mm):
                W("v", t_mm)
                def b(v, c=c, bank=bank):
                    return v.tensor_scalar(kqbf[:, c, :], pk[bank][:, 0:512],
                                           bkf[:, c:c + 1], 0.0, OP.add, OP.add)
                return [G("v", b)]
            proj_fm(lambda kd: h1fm[:, kd, :], ev_k, extra_wait=t_h1fm)
            return tok("v"), t_h1fm

        def pass_a_kp_v(j, i, t_kbf, t_h1fm):
            deps = {}
            for s in range(NSUB):
                t_kp = []
                for half in range(2):
                    W("pe", t_kbf)
                    def b_kp(pe, s=s, half=half):
                        for cc in range(4):
                            c = half * 4 + cc
                            ins = pe.matmul(kpP[:, cc * 128:(cc + 1) * 128],
                                            kqbf[:, c, s * 128:(s + 1) * 128],
                                            rwbd[:, c, :], start=True,
                                            stop=True, skip_group_check=True)
                        return ins
                    t_mm = G("pe", b_kp)
                    W("v", t_mm)
                    def b_kpev(v, s=s, half=half):
                        return v.tensor_scalar(
                            A1[:, s, half * 512:(half + 1) * 512],
                            kpP[:, :], 0.0, EPS, OP.max, OP.add)
                    t_ev = G("v", b_kpev)
                    t_kp.append(t_ev)
                    W("pe", t_ev)
                deps.setdefault(s, {})["kp"] = t_kp

            wap, t_w, u = w_use()
            last_mm = [None]
            for s in range(NSUB):
                t_v = []
                for half in range(2):
                    W("pe", t_w)
                    Wm("pe", t_h1fm)
                    def b_v(pe, s=s, half=half, wap=wap):
                        for kd in range(NCH):
                            pe.matmul(vP[:, :],
                                      h1fm[:, kd, s * 128:(s + 1) * 128],
                                      wap[:, kd, half * 512:(half + 1) * 512],
                                      start=(kd == 0), stop=False)
                        return pe.matmul(vP[:, :], ones128[32:33, :],
                                         bvrow[:, half * 512:(half + 1) * 512],
                                         start=False, stop=True)
                    t_mm = G("pe", b_v)
                    last_mm[0] = t_mm
                    t_h1fm_readers.append(t_mm)
                    W("v", t_mm)
                    def b_vev(v, s=s, half=half):
                        return v.tensor_copy(
                            A2[:, s, half * 4:(half + 1) * 4, 1:129],
                            vP[:, :].rearrange("p (c t) -> p c t", c=4))
                    t_ev = G("v", b_vev)
                    t_v.append(t_ev)
                    W("pe", t_ev)
                deps[s]["v"] = t_v
            w_free_tok[u] = last_mm[0]
            return deps

        def pass_a_kv(j, i, deps):
            for s in range(NSUB):
                Wm("pe", deps[s]["kp"] + deps[s]["v"])
                def b_kv(pe, s=s, i=i):
                    for c in range(NCH):
                        ins = pe.matmul(
                            kvP[:, KVCOL[c]:KVCOL[c] + 129],
                            A1[:, s, c * 128:(c + 1) * 128],
                            A2[:, s, c, :],
                            start=(i == 0 and s == 0 and c in KVSTART),
                            stop=(i == NT - 1 and s == NSUB - 1),
                            skip_group_check=True)
                    return ins
                G("pe", b_kv)

        def finalize_a():
            t_kv = tok("pe")
            W("v", t_kv)
            def b_fin(v):
                for c in range(NCH):
                    col = KVCOL[c]
                    v.tensor_copy(kvm[0:64, c, 0:64], kvP[0:64, col + 1:col + 65])
                    v.tensor_copy(kvm[64:128, c, 64:128],
                                  kvP[64:128, col + 65:col + 129])
                    v.tensor_copy(ksumM[0:64, 2 * c:2 * c + 1],
                                  kvP[0:64, col:col + 1])
                    ins = v.tensor_copy(ksumM[64:128, 2 * c + 1:2 * c + 2],
                                        kvP[64:128, col:col + 1])
                return ins
            return G("v", b_fin)

        def pass_b_s1(j, i, t_diag):
            t_h1fm = ln_tr(j, i, t_diag, passb=True)
            def ev_q(c, bank, t_mm):
                W("s", t_mm)
                def b(sc, c=c, bank=bank):
                    return sc.activation(kqbf[:, c, :], pk[bank][:, 0:512],
                                         AF.Identity, bias=bqf[:, c:c + 1])
                return [G("s", b)]
            proj_fm(lambda kd: h1fm[:, kd, :], ev_q, extra_wait=t_h1fm)

            for c in range(NCH):
                bank = c % 2
                Wm("pe", t_pkfree[bank])
                def b_qp(pe, c=c, bank=bank):
                    return pe.matmul(pk[bank][:, 0:512], rwbd[:, c, :],
                                     kqbf[:, c, :], start=True, stop=True)
                t_mm = G("pe", b_qp)
                W("v", t_mm)
                def b_qpev(v, c=c, bank=bank):
                    return v.tensor_scalar(qpf[:, c, :], pk[bank][:, 0:512],
                                           0.0, EPS, OP.max, OP.add)
                t_pkfree[bank] = [G("v", b_qpev)]
            return tok("v")

        def chain_rstd(st, t_ready, diags=None, wait2=None):
            W("v", t_ready)
            def b_chain(v):
                v.tensor_scalar(st(SC_M), st(SC_SUM), -1.0 / D, None, OP.mult)
                v.drain()
                v.tensor_tensor(st(SC_T), st(SC_M), st(SC_M), OP.mult)
                v.drain()
                v.scalar_tensor_tensor(st(SC_VAR), st(SC_SQ), 1.0 / D,
                                       st(SC_T), OP.mult, OP.subtract)
                return v.drain()
            t_var = G("v", b_chain)
            W("s", t_var)
            t_std = G("s", lambda sc: sc.activation(st(SC_STD), st(SC_VAR),
                                                    AF.Sqrt,
                                                    bias=lneps[:, 0:1]))
            W("v", t_std)
            if wait2:
                Wm("v", wait2)
            def b_rstd(v):
                v.reciprocal(st(SC_RSTD), st(SC_STD))
                v.drain()
                v.scalar_tensor_tensor(st(SC_NEGM), st(SC_M), 1.0,
                                       st(SC_RSTD), OP.mult, OP.mult)
                ins = v.drain()
                if diags is not None:
                    d2ap, dnap = diags
                    v.tensor_scalar(d2ap, idn[:, :], st(SC_RSTD), None,
                                    OP.mult)
                    v.tensor_scalar(dnap, idn[:, :], st(SC_NEGM), None,
                                    OP.mult)
                    ins = v.drain()
                return ins
            return G("v", b_rstd)

        def pass_b_attn(j, i, t_qp, t_kvm_tok, wo, mid_fn=None):
            t_div = {}
            t_x2tok = {}
            t_h2fm = []
            t_ev_sub = {}
            t_xr_s = {}
            t_xk_s = {}

            def attn_sub(s):
                W("pe", t_kvm_tok)
                W("pe", t_qp)
                if s >= 1:
                    W("pe", t_div[s - 1])
                else:
                    Wm("pe", t_pkfree[0] + t_pkfree[1])
                ti_n = trp_next(True)
                W("pe", t_trpfree[ti_n])
                def b_qkv(pe, s=s, ti=ti_n):
                    for c in range(NCH):
                        pe.matmul(pkq[:, c * 128:(c + 1) * 128],
                                  qpf[:, c, s * 128:(s + 1) * 128],
                                  kvm[:, c, :], start=True, stop=True,
                                  skip_group_check=True)
                    for c in range(NCH):
                        ins = pe.matmul(trF32[ti][:, 2 * c:2 * c + 2],
                                        qpf[:, c, s * 128:(s + 1) * 128],
                                        ksumM[:, 2 * c:2 * c + 2],
                                        start=True, stop=True,
                                        skip_group_check=True)
                    return ins
                t_qkv = G("pe", b_qkv)

                W("v", t_qkv)
                def b_recip(v, s=s, ti=ti_n):
                    v.tensor_scalar(recip[:, s, :], trF32[ti][:, 0:16],
                                    EPS, None, OP.add)
                    v.drain()
                    v.reciprocal(recip[:, s, :], recip[:, s, :])
                    return v.drain()
                t_rc = G("v", b_recip)
                t_trpfree[ti_n] = t_rc

                def b_div(v, s=s):
                    return v.tensor_tensor(
                        A1[:, s, :].rearrange("p (h u) -> p h u", h=16),
                        pkq.rearrange("p (h u) -> p h u", h=16),
                        recip[:, s, :].unsqueeze(2).broadcast_to([128, 16, 64]),
                        OP.mult)
                t_div[s] = G("v", b_div)

                ti = trp_next(True)
                W("pe", t_div[s])
                W("pe", t_trpfree[ti])
                def b_otr(pe, s=s, ti=ti):
                    for c in range(NCH):
                        ins = pe.transpose(trBF[ti][:, c * 128:(c + 1) * 128],
                                           A1[:, s, c * 128:(c + 1) * 128],
                                           idn[:, :])
                    return ins
                t_tr = G("pe", b_otr)
                W("v", t_tr)
                def b_oev(v, ti=ti):
                    return v.tensor_copy(
                        odivfm[:, :, :],
                        trBF[ti].rearrange("p (c t) -> p c t", c=NCH))
                t_oev = G("v", b_oev)
                t_trpfree[ti] = t_oev

                sl = s % 2
                W("pe", t_oev)
                Wm("pe", t_bigfree[sl])
                def b_wo(pe, sl=sl, wo=wo):
                    for c in range(NCH):
                        for half in range(2):
                            pe.matmul(bigP(sl)[:, half * 512:(half + 1) * 512],
                                      odivfm[:, c, :],
                                      wo[:, c, half * 512:(half + 1) * 512],
                                      start=(c == 0), stop=False,
                                      skip_group_check=True)
                    for half in range(2):
                        ins = pe.matmul(bigP(sl)[:, half * 512:(half + 1) * 512],
                                        ones128[0:1, :],
                                        borow[:, half * 512:(half + 1) * 512],
                                        start=False, stop=True,
                                        skip_group_check=True)
                    return ins
                return G("pe", b_wo)

            def chain_sub(s, t_wo):
                sl = s % 2
                g_idx = j * NSUB + s
                W("s", tok("v"))
                W("s", t_wo)
                def b_asq(sc, sl=sl):
                    sc.activation(junk[:, 0:1024], bigP(sl), AF.Copy,
                                  accum_out=stA(SC_SUM, 0))
                    sc.drain()
                    sc.activation(junk[:, 0:1024], bigP(sl), AF.Square,
                                  accum_out=stA(SC_SQ, 0))
                    sc.drain()
                    return sc.mul(fence[0:1, 0:1], fence[0:1, 0:1], 1.0)
                t_asq = G("s", b_asq)
                chain_rstd(stA, t_asq)

                W("v", t_x_f[g_idx])
                def b_x2(v, s=s, sl=sl, g_idx=g_idx):
                    v.tensor_scalar(x2[:, s, :], xA[:, g_idx % NXS, :],
                                    stA(SC_NEGM, 0), 0.0, OP.add, OP.add)
                    v.drain()
                    v.scalar_tensor_tensor(x2[:, s, :], bigP(sl),
                                           stA(SC_RSTD, 0), x2[:, s, :],
                                           OP.mult, OP.add,
                                           accum_out=stB(SC_SUM, 0))
                    return v.drain()
                t_o = G("v", b_x2)
                t_xfree_f[g_idx] = t_xfree_f.get(g_idx, []) + [t_o]
                t_bigfree[sl] = [t_o]
                t_x2tok[s] = t_o

                def b_x2sq(v, s=s):
                    v.scalar_tensor_tensor(vjunk[:, :], x2[:, s, :], 1.0,
                                           x2[:, s, :], OP.mult, OP.mult,
                                           accum_out=stB(SC_SQ, 0))
                    return v.drain()
                t_x2sq = G("v", b_x2sq)
                t_h2d = chain_rstd(
                    stB, t_x2sq,
                    diags=(h2d2[:, s % 2, :], h2dn[:, s % 2, :]),
                    wait2=[h2_pe_prev[0]])

                W("v", h2_pe_prev[0])
                def b_cast(v, s=s):
                    return v.tensor_copy(x2bfb[:, s % 2, :], x2[:, s, :])
                t_cast = G("v", b_cast)
                for half in range(2):
                    ti = trp_next(True)
                    W("pe", t_h2d)
                    W("pe", t_cast)
                    W("pe", t_trpfree[ti])
                    def b_h2mm(pe, s=s, ti=ti, half=half):
                        for cc in range(4):
                            c = half * 4 + cc
                            pe.matmul(trF32[ti][:, cc * 128:(cc + 1) * 128],
                                      x2bfb[:, s % 2, c * 128:(c + 1) * 128],
                                      h2d2[:, s % 2, :],
                                      start=(cc == 0), stop=False,
                                      skip_group_check=True)
                        for cc in range(4):
                            ins = pe.matmul(
                                trF32[ti][:, cc * 128:(cc + 1) * 128],
                                ones128[:, :], h2dn[:, s % 2, :],
                                start=False, stop=True, skip_group_check=True)
                        return ins
                    t_tr = G("pe", b_h2mm)
                    h2_pe_prev[0] = t_tr
                    W("s", t_tr)
                    Wm("s", t_mix_prev[0])
                    def b_h2ev(sc, s=s, ti=ti, half=half):
                        return sc.activation(
                            h2fm[:, half * 4:(half + 1) * 4,
                                 1 + s * 128:1 + (s + 1) * 128],
                            trF32[ti].rearrange("p (c t) -> p c t", c=4),
                            AF.Copy)
                    t_ev = G("s", b_h2ev)
                    t_trpfree[ti] = t_ev
                    t_h2fm.append(t_ev)
                    t_ev_sub.setdefault(s, []).append(t_ev)

                Wm("v", t_ev_sub[s])
                if s > 0:
                    Wm("v", t_ev_sub[s - 1])
                Wm("v", t_mix_prev[0])
                Wm("v", t_ffn_free[0])
                lo, hi = s * 128, (s + 1) * 128
                def b_ds(v, s=s, lo=lo, hi=hi):
                    v.tensor_tensor(dfm[:, :, lo:hi],
                                    h2fm[:, :, 1 + lo:1 + hi],
                                    h2fm[:, :, lo:hi], OP.subtract)
                    v.drain()
                    for c in range(NCH):
                        v.scalar_tensor_tensor(xrb[:, c, lo:hi],
                                               dfm[:, c, lo:hi],
                                               murf[:, c:c + 1],
                                               h2fm[:, c, lo:hi],
                                               OP.mult, OP.add)
                    return v.drain()
                t_xr_s[s] = G("v", b_ds)
                W("g", t_xr_s[s])
                Wm("g", t_ffn_free[0])
                def b_xks(g, s=s, lo=lo, hi=hi):
                    g.tensor_tensor(xkv3[:, :, lo:hi], dfm[:, :, lo:hi],
                                    mukf[:, :].unsqueeze(2).broadcast_to(
                                        [128, NCH, 128]), OP.mult)
                    g.drain()
                    return g.tensor_tensor(xkv3[:, :, lo:hi],
                                           xkv3[:, :, lo:hi],
                                           h2fm[:, :, lo:hi], OP.add)
                t_xk_s[s] = G("g", b_xks)

            t_wo = {}
            t_wo[0] = attn_sub(0)
            t_wo[1] = attn_sub(1)
            chain_sub(0, t_wo[0])
            t_wo[2] = attn_sub(2)
            chain_sub(1, t_wo[1])
            t_wo[3] = attn_sub(3)
            chain_sub(2, t_wo[2])
            chain_sub(3, t_wo[3])
            t_pkfree[0] = [t_div[3]]
            t_pkfree[1] = [t_div[3]]
            mid_ret = mid_fn() if mid_fn is not None else None
            Wm("v", [t_xk_s[3]])
            t_mix = [G("v", lambda v: v.tensor_copy(h2fm[:, :, 0:1],
                                                    h2fm[:, :, TT:TT + 1]))]
            return t_h2fm, t_xr_s[3], t_xk_s[3], t_mix, mid_ret

        xkv = A2[:, :, :, :].rearrange("p a b c -> p (a b c)")
        xkv3 = xkv[:, 0:NCH * TT].rearrange("p (c t) -> p c t", c=NCH)
        t_ffn_free = [[]]

        def pass_b_tail(j, i, t_xr, t_xk, t_mix):
            st = stF

            first_r = [True]
            t_v_mark = tok("v")
            def ev_r(c, bank, t_mm):
                W("s", t_mm)
                W("s", t_v_mark)
                def b(sc, c=c, bank=bank):
                    if first_r[0]:
                        sc.preload_act_table(AF.Sigmoid)
                        first_r[0] = False
                    return sc.activation(rbf[:, c, :], pk[bank][:, 0:512],
                                         AF.Sigmoid)
                return [G("s", b)]
            proj_fm(lambda kd: xrb[:, kd, :], ev_r, extra_wait=[t_xr])
            Graw("s", lambda sc: sc.preload_act_table(AF.Sqrt))
            t_last_relu = []

            t_krelu = []
            t_g_mark = tok("g")
            def ev_k2(c, bank, t_mm):
                W("s", t_mm)
                W("s", t_g_mark)
                def b(sc, c=c, bank=bank):
                    return sc.activation(xrb[:, c, :], pk[bank][:, 0:512],
                                         AF.Relu)
                t = G("s", b)
                t_krelu.append(t)
                t_last_relu[:] = [t]
                return [t]
            proj_fm(lambda kd: xkv[:, kd * TT:(kd + 1) * TT], ev_k2,
                    extra_wait=[t_xk])
            Wm("g", t_krelu)
            W("g", tok("v"))
            def b_sqr(g):
                for c in range(NCH):
                    ins = g.tensor_tensor(dfm[:, c, :], xrb[:, c, :],
                                          xrb[:, c, :], OP.mult)
                return ins
            t_kact = G("g", b_sqr)

            def ev_f(c, bank, t_mm):
                W("v", t_mm)
                def b(v, c=c, bank=bank):
                    return v.tensor_tensor(ffm[:, c, :], pk[bank][:, 0:512],
                                           rbf[:, c, :], OP.mult)
                return [G("v", b)]
            proj_fm(lambda kd: dfm[:, kd, :], ev_f, extra_wait=[t_kact])
            t_f = tok("v")
            t_ffn_free[0] = [tok("pe"), t_kact] + t_last_relu

            for p in range(2):
                t_fsq = None
                for q_ in range(2):
                    s = p * 2 + q_
                    ti = trp_next(True)
                    W("pe", t_f)
                    W("pe", t_trpfree[ti])
                    def b_ftr(pe, s=s, ti=ti):
                        for c in range(NCH):
                            ins = pe.transpose(
                                trBF[ti][:, c * 128:(c + 1) * 128],
                                ffm[:, c, s * 128:(s + 1) * 128], idn[:, :])
                        return ins
                    t_tr = G("pe", b_ftr)
                    W("s", t_tr)
                    W("s", tok("v"))
                    def b_fev(sc, s=s, ti=ti, q_=q_):
                        return sc.activation(A1[:, s, :], trBF[ti][:, :],
                                             AF.Copy, accum_out=st(SC_SUM, q_))
                    t_ev = G("s", b_fev)
                    t_trpfree[ti] = t_ev
                    def b_fsq(sc, s=s, q_=q_):
                        sc.drain()
                        sc.activation(junk[:, 0:1024], A1[:, s, :], AF.Square,
                                      accum_out=st(SC_SQ, q_))
                        sc.drain()
                        return sc.mul(fence[0:1, 0:1], fence[0:1, 0:1], 1.0)
                    t_fsq = G("s", b_fsq)
                chain_rstd(st, t_fsq)

                for q_ in range(2):
                    s = p * 2 + q_
                    sidx = i * NSUB + s
                    slot = sidx % 2
                    W("v", t_outfree.get(sidx - 2))
                    def b_out(v, s=s, q_=q_, slot=slot):
                        v.tensor_scalar(outS[:, slot, :], x2[:, s, :],
                                        st(SC_NEGM, q_), 0.0, OP.add, OP.add)
                        v.drain()
                        return v.scalar_tensor_tensor(
                            outS[:, slot, :], A1[:, s, :],
                            st(SC_RSTD, q_), outS[:, slot, :],
                            OP.mult, OP.add)
                    t_o = G("v", b_out)
                    W("g", t_o)
                    def b_st(g, sidx=sidx, slot=slot):
                        return g.dma_start(osv[:, sidx, :], outS[:, slot, :])
                    t_outfree[sidx] = G("g", b_st, sem="st%d" % slot, n=16)
            return t_mix

        for E in ("pe", "v", "s"):
            Wm(E, t_setup)

        w_load(3)
        load_tile(0, 0)
        t_rst = ln_chain(0, 0)
        t_dg = mk_diag(0, t_rst)
        t_kbf, t_h1fm = pass_a_s1(0, 0, t_dg)
        for i in range(NT):
            if i + 1 < NT:
                load_tile(i + 1, i + 1)
                t_rst = ln_chain(i + 1, i + 1)
                t_dg = mk_diag(i + 1, t_rst)
            if i == 0:
                dump("h1fmA", h1fm[:, :, :].rearrange("p a b -> p (a b)"),
                     tok("s"))
                dump("kbf", kqbf[:, :, :].rearrange("p a b -> p (a b)"),
                     tok("v"))
            deps = pass_a_kp_v(i, i, t_kbf, t_h1fm)
            if i == 0:
                dump("kp", A1[:, :, :].rearrange("p a b -> p (a b)"), tok("v"))
                dump("vext", A2[:, :, :, :].rearrange("p a b c -> p (a b c)"),
                     tok("v"))
            pass_a_kv(i, i, deps)
            if i + 1 < NT:
                t_kbf, t_h1fm = pass_a_s1(i + 1, i + 1, t_dg)

        t_kvm_tok = finalize_a()
        dump("kvm", kvm[:, :, :].rearrange("p a b -> p (a b)"), t_kvm_tok)
        dump("ksumM", ksumM[:, :], t_kvm_tok)
        t_trpfree[1] = t_kvm_tok

        load_tile(NT, 0)
        t_dg = mk_diag(0, t_kvm_tok)
        t_qp = pass_b_s1(NT, 0, t_dg)
        for i in range(NT):
            if i + 1 < NT:
                load_tile_bf(NT + i + 1, i + 1)
                t_dg = mk_diag(i + 1, t_kvm_tok)
            wo, t_wo, u_wo = w_use()
            W("pe", t_wo)
            if i == 0:
                dump("qpf", qpf[:, :, :].rearrange("p a b -> p (a b)"),
                     tok("v"))
            def mid_fn(i=i, t_dg=t_dg):
                if i + 1 >= NT:
                    return None
                return pass_b_s1(NT + i + 1, i + 1, t_dg)
            t_h2fm_all, t_xr3, t_xk3, t_mix, t_qp_next = pass_b_attn(
                NT + i, i, t_qp, t_kvm_tok, wo, mid_fn)
            w_free_tok[u_wo] = tok("pe")
            if i + 1 < NT:
                load_tile_f32(NT + i + 1, i + 1)
            if i == 0:
                dump("x2", x2[:, :, :].rearrange("p a b -> p (a b)"), tok("v"))
                dump("h2fm", h2fm[:, :, :].rearrange("p a b -> p (a b)"),
                     tok("s"))
            if i + 1 < NT:
                t_qp = t_qp_next
            pass_b_tail(NT + i, i, t_xr3, t_xk3, t_mix)
            t_mix_prev[0] = t_mix
            if i == 0:
                dump("ffm", ffm[:, :, :].rearrange("p a b -> p (a b)"),
                     tok("v"))
                dump("ftm", A1[:, :, :].rearrange("p a b -> p (a b)"),
                     tok("s"))

        def g_end(g):
            g.wait_ge(sems["st0"], 16 * (NT * NSUB // 2))
            g.wait_ge(sems["st1"], 16 * (NT * NSUB // 2))
            if dbg:
                g.wait_ge(sems["wt"], cnt["wt"])
        Graw("g", g_end)

        block = e(nc.Block())

        def run(E):
            def body(eng):
                for f in q[E]:
                    f(eng)
            return body
        block.tensor(run("pe"))
        block.vector(run("v"))
        block.scalar(run("s"))
        block.gpsimd(run("g"))
        block.sync(run("sy"))

    return nc


_prog_cache = {}


def _get_prog(S, NT):
    key = (S, NT)
    if key not in _prog_cache:
        _prog_cache[key] = build_program(S, NT)
    return _prog_cache[key]


def kernel(L0_attn_x, L0_attn_Wq, L0_attn_bq, L0_attn_Wk, L0_attn_bk,
           L0_attn_Wv, L0_attn_bv, L0_attn_Wo, L0_attn_bo, L0_attn_rw,
           L0_ffn_W_r, L0_ffn_W_k, L0_ffn_W_v, L0_ffn_mu_r, L0_ffn_mu_k):
    xin = np.asarray(L0_attn_x)
    Bb, S, Dd = xin.shape
    assert Dd == D and S % TT == 0
    NT = S // TT
    nc = _get_prog(S, NT)

    com = {
        "wq": np.ascontiguousarray(L0_attn_Wq, np.float32),
        "wk": np.ascontiguousarray(L0_attn_Wk, np.float32),
        "wv": np.ascontiguousarray(L0_attn_Wv, np.float32),
        "wo": np.ascontiguousarray(L0_attn_Wo, np.float32),
        "wr": np.ascontiguousarray(np.asarray(L0_ffn_W_r, np.float32).T),
        "wkf": np.ascontiguousarray(np.asarray(L0_ffn_W_k, np.float32).T),
        "wvf": np.ascontiguousarray(np.asarray(L0_ffn_W_v, np.float32).T),
        "rw": np.ascontiguousarray(L0_attn_rw, np.float32),
        "bq": np.ascontiguousarray(L0_attn_bq, np.float32),
        "bk": np.ascontiguousarray(L0_attn_bk, np.float32),
        "bv": np.ascontiguousarray(L0_attn_bv, np.float32),
        "bo": np.ascontiguousarray(L0_attn_bo, np.float32),
        "mur": np.ascontiguousarray(L0_ffn_mu_r, np.float32),
        "muk": np.ascontiguousarray(L0_ffn_mu_k, np.float32),
    }
    in_maps = []
    for b in range(Bb):
        m = dict(com)
        m["x"] = np.ascontiguousarray(xin[b], np.float32)
        in_maps.append(m)

    res = run_bass_kernel_spmd(nc, in_maps, core_ids=list(range(Bb)))
    out = np.stack([r["out"] for r in res.results], axis=0)
    return out.astype(np.float32)
